# version 01 (final)
# baseline (speedup 1.0000x reference)
import contextlib
import numpy as np
import ml_dtypes
import concourse.bass as bass
import concourse.mybir as mybir
from concourse.bass_utils import run_bass_kernel_spmd

F32 = mybir.dt.float32
BF16 = mybir.dt.bfloat16
AF = mybir.ActivationFunctionType
ALU = mybir.AluOpType
ET = mybir.EngineType
AX = mybir.AxisListType

MAGIC = float(1.5 * 2 ** 23)
D = 1024
FF = 4096
T = 2048
S = 8192
NDC = 8
NFC = 32
TT = 512
NTT = 4
LNTILES = 4
NH = 16
SCALE = 1.0 / 32.0
GROUPS = [[0, 1, 2, 3], [4, 5, 6, 7]]
STAGE = 3
QCH = 1
QSKIP = "full"
LNTILES = 4
LNPART = 8
LNMUT = "none"

ENGS = ("sync", "tensor", "vector", "scalar", "gpsimd")


class Sem:
    def __init__(self, nc, name):
        self.h = nc.alloc_semaphore(name)
        self.n = 0


class B:

    def __init__(self):
        self.nc = bass.Bass("TRN2", target_bir_lowering=False, debug=False)
        self.plan = {e: [] for e in ENGS}

    def ei(self, eng, fn, wait=None, inc=None, inc_n=1):
        if inc is not None:
            inc.n += inc_n
        tgt = inc.n if inc is not None else None

        def run(e, fn=fn, wait=wait, inc=inc, inc_n=inc_n):
            if wait is not None:
                e.wait_ge(wait[0].h, wait[1])
            inst = fn(e)
            if inc is not None:
                inst.then_inc(inc.h, inc_n)
        self.plan[eng].append(run)
        return tgt

    def wt(self, eng, sem, val):
        self.plan[eng].append(lambda e, s=sem.h, v=val: e.wait_ge(s, v))

    def dma(self, eng, out_ap, in_ap, sem, wait=None):
        return self.ei(eng, lambda e, o=out_ap, i=in_ap: e.dma_start(out=o, in_=i),
                       wait=wait, inc=sem, inc_n=16)


def build():
    bb = B()
    nc = bb.nc
    st = contextlib.ExitStack()

    x_fm = nc.dram_tensor("x_fm", [D, T], F32, kind="ExternalInput").ap()
    W1 = nc.dram_tensor("W1", [FF, D], F32, kind="ExternalInput").ap()
    W2 = nc.dram_tensor("W2", [D, FF], F32, kind="ExternalInput").ap()
    b1_rs = nc.dram_tensor("b1_rs", [128, 32], F32, kind="ExternalInput").ap()
    b2_rs = nc.dram_tensor("b2_rs", [128, 8], F32, kind="ExternalInput").ap()
    wk_st = nc.dram_tensor("wk_st", [D, D], F32, kind="ExternalInput").ap()
    wq_st = nc.dram_tensor("wq_st", [D, D], F32, kind="ExternalInput").ap()
    wv_st = nc.dram_tensor("wv_st", [D, D], F32, kind="ExternalInput").ap()
    wo_st = nc.dram_tensor("wo_st", [D, D], F32, kind="ExternalInput").ap()
    ident = nc.dram_tensor("ident", [128, 128], BF16, kind="ExternalInput").ap()
    ones128 = nc.dram_tensor("ones128", [128, 128], BF16, kind="ExternalInput").ap()
    ones32 = nc.dram_tensor("ones32", [128, 32], BF16, kind="ExternalInput").ap()
    ones64A = nc.dram_tensor("ones64A", [128, 32], BF16, kind="ExternalInput").ap()
    ones64B = nc.dram_tensor("ones64B", [128, 32], BF16, kind="ExternalInput").ap()
    mask01 = nc.dram_tensor("mask01", [128, 128], BF16, kind="ExternalInput").ap()
    selbc = nc.dram_tensor("selbc", [64, 128], F32, kind="ExternalInput").ap()
    onesbc = nc.dram_tensor("onesbc", [1, 128], F32, kind="ExternalInput").ap()
    onescol = nc.dram_tensor("onescol", [128, 1], F32, kind="ExternalInput").ap()
    out_fm = nc.dram_tensor("out_fm", [D, T], F32, kind="ExternalOutput").ap()

    wq1t = nc.dram_tensor("wq1t", [NFC * 128, NDC * 128], BF16).ap()
    wq2t = nc.dram_tensor("wq2t", [NDC * 128, NFC * 128], BF16).ap()
    wkb = nc.dram_tensor("wkb", [D, D], BF16).ap()
    wqb = nc.dram_tensor("wqb", [D, D], BF16).ap()
    wvb = nc.dram_tensor("wvb", [D, D], BF16).ap()
    wob = nc.dram_tensor("wob", [D, D], BF16).ap()
    kq_piece = nc.dram_tensor("kq_piece", [2 * D, T], BF16).ap()
    v_piece = nc.dram_tensor("v_piece", [T, D], BF16).ap()
    kq_gath = nc.dram_tensor("kq_gath", [4 * 2 * D, T], BF16).ap()
    v_gath = nc.dram_tensor("v_gath", [S, D], BF16).ap()
    rs_in = nc.dram_tensor("rs_in", [4 * D, T], F32).ap()
    rs_out = nc.dram_tensor("rs_out", [D, T], F32).ap()

    xr = st.enter_context(nc.sbuf_tensor("xr", [128, NDC * T], F32))
    ybf = st.enter_context(nc.sbuf_tensor("ybf", [128, NDC * T], BF16))
    bufA = st.enter_context(nc.sbuf_tensor("bufA", [128, 16384], BF16))
    bufB = st.enter_context(nc.sbuf_tensor("bufB", [128, 16384], BF16))
    stat = st.enter_context(nc.sbuf_tensor("stat", [128, 4096], F32))
    fsm = st.enter_context(nc.sbuf_tensor("fsm", [128, 512], F32))
    bsm = st.enter_context(nc.sbuf_tensor("bsm", [128, 512], BF16))
    wstg = st.enter_context(nc.sbuf_tensor("wstg", [128, 2048], F32))
    ps = [st.enter_context(nc.psum_tensor(f"ps{i}", [128, 512], F32)) for i in range(8)]

    dS = Sem(nc, "dS")
    dG = Sem(nc, "dG")
    dw = [Sem(nc, f"dw{i}") for i in range(6)]
    dg2 = [Sem(nc, f"dg{i}") for i in range(2)]
    dstq = Sem(nc, "dstq")
    dstc = Sem(nc, "dstc")
    cP = Sem(nc, "cP")
    cA = Sem(nc, "cA")
    cV = Sem(nc, "cV")
    colKQ = Sem(nc, "colKQ")
    colV = Sem(nc, "colV")
    colO = Sem(nc, "colO")

    ei, wt, dma = bb.ei, bb.wt, bb.dma


    for dc in range(NDC):
        dma("sync", xr[:, 2048 * dc:2048 * dc + 2048], x_fm[128 * dc:128 * dc + 128, :], dS)
    dma("sync", fsm[:, 0:32], b1_rs, dS)
    dma("sync", fsm[:, 32:40], b2_rs, dS)
    dma("sync", fsm[0:64, 64:192], selbc, dS)
    dma("sync", fsm[0:1, 192:320], onesbc, dS)
    dma("sync", fsm[:, 320:321], onescol, dS)
    dma("sync", bsm[:, 0:128], ident, dS)
    dma("sync", bsm[:, 128:256], ones128, dS)
    dma("sync", bsm[:, 256:288], ones32, dS)
    dma("sync", bsm[:, 288:320], ones64A, dS)
    dma("sync", bsm[:, 320:352], ones64B, dS)
    dma("sync", bsm[:, 352:480], mask01, dS)
    init_loaded = dS.n

    ei("vector", lambda e: e.memset(fsm[:, 322:323], 1e-5), wait=(dS, init_loaded))
    b2h_done = ei("vector", lambda e: e.tensor_scalar_mul(fsm[:, 40:48], fsm[:, 32:40], 0.5),
                  inc=cV)

    slotdep = {}

    def LN(t0):
        wt("tensor", cA, slotdep.get("ps4", 0))
        for dc in range(NDC):
            xb = 12288 + 512 * (dc % 2)
            v = ei("vector", lambda e, dc=dc, xb=xb, t0=t0: e.tensor_copy(
                bufB[:, xb:xb + 512], xr[:, 2048 * dc + t0:2048 * dc + t0 + 512]),
                wait=(cP, slotdep.get(("xb", dc % 2), 0)), inc=cV)
            p = ei("tensor", lambda e, dc=dc, xb=xb: e.matmul(
                ps[4][:, :], bsm[:, 128:256], bufB[:, xb:xb + 512],
                start=(dc == 0), stop=(dc == NDC - 1), skip_group_check=True),
                wait=(cV, v), inc=cP)
            slotdep[("xb", dc % 2)] = p
        p_mean = cP.n
        if LNPART < 2:
            return cV.n
        a_mean = ei("scalar", lambda e: e.activation(
            stat[:, 0:512], ps[4][:, :],
            AF.Identity if LNMUT == "ident" else AF.Copy,
            scale=1.0 if LNMUT == "scale1" else 1.0 / 1024.0),
            wait=(cP, p_mean + (4 if LNMUT == "delay" else 0)), inc=cA)
        if LNPART < 3:
            return cV.n
        wt("vector", cA, a_mean)
        for dc in range(NDC):
            sq = 13312 + 512 * (dc % 2)
            v = ei("vector", lambda e, dc=dc, t0=t0: e.tensor_sub(
                ybf[:, 2048 * dc + t0:2048 * dc + t0 + 512],
                xr[:, 2048 * dc + t0:2048 * dc + t0 + 512], stat[:, 0:512]), inc=cV)
            wt("scalar", cP, slotdep.get(("sq", dc % 2), 0))
            a = ei("scalar", lambda e, dc=dc, sq=sq, t0=t0: e.activation(
                bufB[:, sq:sq + 512], ybf[:, 2048 * dc + t0:2048 * dc + t0 + 512],
                AF.Square), wait=(cV, v), inc=cA)
            p = ei("tensor", lambda e, sq=sq, dc=dc: e.matmul(
                ps[4][:, :], bsm[:, 128:256], bufB[:, sq:sq + 512],
                start=(dc == 0), stop=(dc == NDC - 1), skip_group_check=True),
                wait=(cA, a), inc=cP)
            slotdep[("sq", dc % 2)] = p
        p_var = cP.n
        if LNPART < 4:
            return cV.n
        a_sd = ei("scalar", lambda e: e.activation(
            stat[:, 512:1024], ps[4][:, :], AF.Sqrt, bias=fsm[:, 322:323],
            scale=1.0 / 1024.0), wait=(cP, p_var), inc=cA)
        slotdep["ps4"] = a_sd
        if LNPART < 5:
            return cV.n
        vr = ei("vector", lambda e: e.reciprocal(stat[:, 512:1024], stat[:, 512:1024]),
                wait=(cA, a_sd), inc=cV)
        for dc in range(NDC):
            ei("vector", lambda e, dc=dc, t0=t0: e.tensor_mul(
                ybf[:, 2048 * dc + t0:2048 * dc + t0 + 512],
                ybf[:, 2048 * dc + t0:2048 * dc + t0 + 512], stat[:, 512:1024]),
                wait=(cV, vr) if dc == 0 else None,
                inc=(cV if dc == NDC - 1 else None))
        return cV.n

    def run_LN():
        done = 0
        for ti in range(LNTILES):
            done = LN(ti * TT)
        return done

    def emit_gamma():
        wt("sync", dw[4], dw[4].n)
        wt("sync", dw[5], dw[5].n)
        for wi in range(2):
            vlast = 0
            for ch in range(32):
                slot = 1024 * (ch % 2)
                if wi == 0:
                    src = W1[128 * ch:128 * ch + 128, :]
                else:
                    src = W2[128 * (ch // 4):128 * (ch // 4) + 128,
                             1024 * (ch % 4):1024 * (ch % 4) + 1024]
                d = dma("sync", wstg[:, slot:slot + 1024], src, dw[4 + ch % 2],
                        wait=(cV, slotdep.get(("wstg", ch % 2), 0)))
                vlast = ei("vector", lambda e, slot=slot, ch=ch: e.reduce_sum(
                    stat[:, 2048 + 8 * ch:2048 + 8 * ch + 8],
                    wstg[:, slot:slot + 1024].rearrange("p (a b) -> p a b", a=8),
                    AX.X, apply_absolute_value=True),
                    wait=(dw[4 + ch % 2], d), inc=cV)
                slotdep[("wstg", ch % 2)] = vlast
            vf = ei("vector", lambda e: e.reduce_sum(
                stat[:, 3120:3121], stat[:, 2048:2304], AX.X),
                wait=(cV, vlast), inc=cV)
            v = ei("vector", lambda e: e.tensor_copy(fsm[:, 324:325], stat[:, 3120:3121]),
                   wait=(cV, vf), inc=cV)
            p = ei("tensor", lambda e: e.matmul(
                ps[5][0:1, 0:1], fsm[:, 324:325], fsm[:, 320:321],
                start=True, stop=True, skip_group_check=True), wait=(cV, v), inc=cP)
            vts = ei("vector", lambda e: e.tensor_scalar(
                stat[0:1, 3121:3122], ps[5][0:1, 0:1], 1.0 / (FF * D), 1e-5,
                ALU.mult, ALU.add), wait=(cP, p), inc=cV)
            v = ei("vector", lambda e: e.reciprocal(stat[0:1, 3121:3122],
                                                    stat[0:1, 3121:3122]),
                   wait=(cV, vts), inc=cV)
            p = ei("tensor", lambda e: e.matmul(
                ps[5][:, 1:2], fsm[0:1, 192:320], stat[0:1, 3121:3122],
                start=True, stop=True, skip_group_check=True), wait=(cV, v), inc=cP)
            ei("vector", lambda e, wi=wi: e.tensor_copy(fsm[:, 48 + wi:49 + wi],
                                                        ps[5][:, 1:2]),
               wait=(cP, p), inc=cV)
        return cV.n

    def emit_quant():
        wt("sync", dw[4], dw[4].n)
        wt("sync", dw[5], dw[5].n)
        for wi in ((0,) if STAGE == "quant1" else (0, 1)):
            nchunks = NFC if wi == 0 else NDC
            if STAGE == "quant1":
                nchunks = QCH
            width = 1024 if wi == 0 else 4096
            npieces = width // 1024
            for ch in range(nchunks):
                stg = 8192 if wi == 0 else 12288 - 4096
                stg = 8192
                for pc in range(npieces):
                    slot = 1024 * (pc % 2)
                    qcol = (2048 * (ch % 2)) if wi == 0 else 1024 * pc
                    qbase = 0 if wi == 0 else 4096
                    src = (W1 if wi == 0 else W2)[128 * ch:128 * ch + 128,
                                                  1024 * pc:1024 * pc + 1024]
                    d = dma("sync", wstg[:, slot:slot + 1024], src, dw[4 + pc % 2],
                            wait=(cV, slotdep.get(("wstg", pc % 2), 0)))
                    wt("vector", cV, slotdep.get("qtmp", 0))
                    v1 = ei("vector", lambda e, slot=slot, wi=wi, pc=pc: e.tensor_scalar(
                        stat[:, 2048:3072], wstg[:, slot:slot + 1024],
                        fsm[:, 48 + wi:49 + wi], MAGIC, ALU.mult, ALU.add),
                        wait=(dw[4 + pc % 2], d), inc=cV)
                    vv = ei("vector", lambda e: e.tensor_scalar(
                        stat[:, 2048:3072], stat[:, 2048:3072], -MAGIC, -1.0,
                        ALU.add, ALU.max), wait=(cV, v1), inc=cV)
                    slotdep[("wstg", pc % 2)] = vv
                    wt("vector", cP, slotdep.get(("qreg", wi, ch % 2 if wi == 0 else 0), 0))
                    v = ei("vector", lambda e, qbase=qbase, qcol=qcol: e.tensor_scalar_min(
                        bufB[:, qbase + qcol:qbase + qcol + 1024], stat[:, 2048:3072], 1.0),
                        wait=(cV, vv), inc=cV)
                    slotdep["qtmp"] = v
                    if pc == 0:
                        wt("scalar", dstq, slotdep.get("qst", 0))
                    wt("tensor", cA, slotdep.get(("pst", wi), 0))
                    p = None
                    for db in range(8):
                        gdb = pc * 8 + db
                        pb = ps[6 + (gdb // 4) % 2]
                        pofs = 128 * (gdb % 4)
                        p = ei("tensor", lambda e, pb=pb, pofs=pofs, qbase=qbase, qcol=qcol,
                               db=db: e.matmul(
                                   pb[:, pofs:pofs + 128],
                                   bufB[:, qbase + qcol + 128 * db:qbase + qcol + 128 * db + 128],
                                   bsm[:, 0:128], start=True, stop=True,
                                   skip_group_check=True),
                               wait=(cV, v) if db == 0 else None,
                               inc=(cP if db == 7 else None))
                        slotdep[("qreg", wi, ch % 2 if wi == 0 else 0)] = p
                    wt("scalar", cP, p)
                    a = None
                    for db in range(8):
                        gdb = pc * 8 + db
                        pb = ps[6 + (gdb // 4) % 2]
                        pofs = 128 * (gdb % 4)
                        a = ei("scalar", lambda e, pb=pb, pofs=pofs, stg=stg, gdb=gdb: e.copy(
                            bufB[:, stg + 128 * gdb:stg + 128 * gdb + 128],
                            pb[:, pofs:pofs + 128]),
                            inc=(cA if db == 7 else None))
                    slotdep[("pst", wi)] = a
                a_sem = cV if QSKIP == "evac2" else cA
                a_last = a_sem.n
                if QSKIP not in ("trans", "store", "evac"):
                    dst = (wq1t if wi == 0 else wq2t)[128 * ch:128 * ch + 128, 0:width]
                    dma("sync", dst, bufB[:, stg:stg + width], dstq, wait=(a_sem, a_last))
                    slotdep["qst"] = dstq.n
        return dstq.n

    def emit_FFN(y_ready):
        for i in range(6):
            wt("sync", dw[i], dw[i].n)
        first_wait_done = [False]
        for ti in range(NTT):
            t0 = ti * TT
            for fc in range(NFC):
                ws = 1024 * (fc % 4)
                d = dma("sync", bufB[:, ws:ws + 1024], wq1t[128 * fc:128 * fc + 128, :],
                        dw[fc % 4], wait=(cP, slotdep.get(("wq1", fc % 4), 0)))
                pb = ps[fc % 2]
                if not first_wait_done[0]:
                    wt("tensor", cV, y_ready)
                    first_wait_done[0] = True
                wt("tensor", cA, slotdep.get(("psF", fc % 2), 0))
                for dc in range(NDC):
                    p = ei("tensor", lambda e, pb=pb, ws=ws, dc=dc, t0=t0: e.matmul(
                        pb[:, :], bufB[:, ws + 128 * dc:ws + 128 * dc + 128],
                        ybf[:, 2048 * dc + t0:2048 * dc + t0 + 512],
                        start=(dc == 0), stop=(dc == NDC - 1), skip_group_check=True),
                        wait=(dw[fc % 4], d) if dc == 0 else None,
                        inc=(cP if dc == NDC - 1 else None))
                slotdep[("wq1", fc % 4)] = cP.n
                a = ei("scalar", lambda e, pb=pb, fc=fc: e.activation(
                    bufA[:, 512 * fc:512 * fc + 512], pb[:, :], AF.Relu,
                    bias=fsm[:, fc:fc + 1], scale=1.0), wait=(cP, cP.n), inc=cA)
                slotdep[("psF", fc % 2)] = a
            for dc in range(NDC):
                ws2 = 4096 + 4096 * (dc % 2)
                d = dma("sync", bufB[:, ws2:ws2 + 4096], wq2t[128 * dc:128 * dc + 128, :],
                        dw[4 + dc % 2], wait=(cP, slotdep.get(("wq2", dc % 2), 0)))
                pb = ps[2 + (dc % 2)]
                wt("tensor", cA, cA.n if dc == 0 else slotdep.get(("psG", dc % 2), 0))
                for fc in range(NFC):
                    p = ei("tensor", lambda e, pb=pb, ws2=ws2, fc=fc: e.matmul(
                        pb[:, :], bufB[:, ws2 + 128 * fc:ws2 + 128 * fc + 128],
                        bufA[:, 512 * fc:512 * fc + 512],
                        start=(fc == 0), stop=(fc == NFC - 1), skip_group_check=True),
                        wait=(dw[4 + dc % 2], d) if fc == 0 else None,
                        inc=(cP if fc == NFC - 1 else None))
                slotdep[("wq2", dc % 2)] = cP.n
                tsl = 1024 + 512 * (dc % 2)
                wt("scalar", cV, slotdep.get(("tsl", dc % 2), 0))
                a = ei("scalar", lambda e, pb=pb, dc=dc, tsl=tsl: e.activation(
                    stat[:, tsl:tsl + 512], pb[:, :], AF.Identity,
                    bias=fsm[:, 40 + dc:41 + dc], scale=0.5), wait=(cP, cP.n), inc=cA)
                slotdep[("psG", dc % 2)] = a
                v = ei("vector", lambda e, dc=dc, t0=t0, tsl=tsl: e.tensor_add(
                    xr[:, 2048 * dc + t0:2048 * dc + t0 + 512],
                    xr[:, 2048 * dc + t0:2048 * dc + t0 + 512],
                    stat[:, tsl:tsl + 512]), wait=(cA, a), inc=cV)
                slotdep[("tsl", dc % 2)] = v
        return cV.n

    def cast_w(src, dst):
        wt("sync", dw[4], dw[4].n)
        wt("sync", dw[5], dw[5].n)
        for cb in range(8):
            slot = 1024 * (cb % 2)
            d = dma("sync", wstg[:, slot:slot + 1024], src[128 * cb:128 * cb + 128, :],
                    dw[4 + cb % 2], wait=(cV, slotdep.get(("wstg", cb % 2), 0)))
            wt("vector", dstc, dstc.n)
            v = ei("vector", lambda e, slot=slot, cb=cb: e.tensor_copy(
                bufB[:, 12288 + slot:12288 + slot + 1024], wstg[:, slot:slot + 1024]),
                wait=(dw[4 + cb % 2], d), inc=cV)
            slotdep[("wstg", cb % 2)] = v
            dma("sync", dst[128 * cb:128 * cb + 128, :],
                bufB[:, 12288 + slot:12288 + slot + 1024], dstc, wait=(cV, v))
            slotdep[("cwst", cb % 2)] = dstc.n
        return dstc.n

    wt("vector", cV, b2h_done)
    y1_ready = cV.n if QSKIP == "noln" else run_LN()
    if STAGE == "ln":
        emit_output_v = y1_ready
    if QSKIP == "serial":
        wt("tensor", cV, cV.n)
        wt("tensor", cA, cA.n)
        wt("scalar", cV, cV.n)
        wt("scalar", cP, cP.n)
        wt("vector", cP, cP.n)
        wt("vector", cA, cA.n)
        wt("sync", cV, cV.n)
    if STAGE not in ("ln",):
        if LNMUT in ("nogamma", "notrail"):
            ei("vector", lambda e: e.memset(fsm[:, 48:50], 0.7), inc=cV)
            gamma_v = cV.n
        else:
            gamma_v = emit_gamma()
        if STAGE == "gamma":
            emit_output_v = gamma_v
    if STAGE not in ("ln", "gamma"):
        quant_done = emit_quant()
        if STAGE in ("quant", "quant1"):
            if quant_done and LNMUT != "notrail":
                wt("vector", dstq, quant_done)
            if LNMUT != "notrail":
                wt("vector", cA, cA.n)
            emit_output_v = cV.n
    if STAGE in ("ln", "gamma", "quant", "quant1"):
        wt("sync", cV, emit_output_v)
        for dc in range(NDC):
            dma("sync", out_fm[128 * dc:128 * dc + 128, :],
                xr[:, 2048 * dc:2048 * dc + 2048], dS)
        wt("sync", dS, dS.n)
        finalize(bb, st)
        return nc
    wt("sync", dstq, quant_done)
    x1_done = emit_FFN(y1_ready)

    def emit_output(ready_v):
        wt("sync", cV, ready_v)
        for dc in range(NDC):
            dma("sync", out_fm[128 * dc:128 * dc + 128, :],
                xr[:, 2048 * dc:2048 * dc + 2048], dS)
        wt("sync", dS, dS.n)

    if STAGE == 1:
        emit_output(x1_done)
        finalize(bb, st)
        return nc

    wt("vector", cV, x1_done)
    y2_ready = run_LN()

    wt("vector", cP, cP.n)
    cast_w(wk_st, wkb)
    cast_w(wq_st, wqb)
    cast_w(wv_st, wvb)
    wcast_done = cast_w(wo_st, wob)

    def kq_proj(w_dram, row_base, y_ready):
        wt("sync", dw[0], dw[0].n)
        wt("sync", dw[1], dw[1].n)
        wt("gpsimd", dg2[0], dg2[0].n)
        wt("gpsimd", dg2[1], dg2[1].n)
        for cb in range(8):
            ws = 1024 * (cb % 2)
            d = dma("sync", bufB[:, ws:ws + 1024], w_dram[128 * cb:128 * cb + 128, :],
                    dw[cb % 2], wait=(cP, slotdep.get(("wkq", cb % 2), 0)))
            for ti in range(NTT):
                t0 = ti * TT
                pb = ps[(cb * NTT + ti) % 2]
                wt("tensor", cA, slotdep.get(("psKQ", (cb * NTT + ti) % 2), 0))
                for dc in range(NDC):
                    ei("tensor", lambda e, pb=pb, ws=ws, dc=dc, t0=t0: e.matmul(
                        pb[:, :], bufB[:, ws + 128 * dc:ws + 128 * dc + 128],
                        ybf[:, 2048 * dc + t0:2048 * dc + t0 + 512],
                        start=(dc == 0), stop=(dc == NDC - 1), skip_group_check=True),
                        wait=(dw[cb % 2], d) if dc == 0 else None,
                        inc=(cP if dc == NDC - 1 else None))
                slotdep[("wkq", cb % 2)] = cP.n
                ss = 10240 + 512 * ((cb * NTT + ti) % 2)
                wt("scalar", dg2[(cb * NTT + ti) % 2],
                   slotdep.get(("sskq", (cb * NTT + ti) % 2), 0))
                a = ei("scalar", lambda e, pb=pb, ss=ss: e.copy(bufB[:, ss:ss + 512],
                                                                pb[:, :]),
                       wait=(cP, cP.n), inc=cA)
                slotdep[("psKQ", (cb * NTT + ti) % 2)] = a
                g = dma("gpsimd",
                        kq_piece[row_base + 128 * cb:row_base + 128 * cb + 128, t0:t0 + 512],
                        bufB[:, ss:ss + 512], dg2[(cb * NTT + ti) % 2], wait=(cA, a))
                slotdep[("sskq", (cb * NTT + ti) % 2)] = g
        return dG.n

    wt("sync", dstc, wcast_done)
    wt("tensor", cV, y2_ready)
    slotdep[("wkq", 0)] = cP.n
    slotdep[("wkq", 1)] = cP.n
    kq_proj(wkb, 0, y2_ready)
    kq_proj(wqb, D, y2_ready)

    wt("gpsimd", dg2[0], dg2[0].n)
    wt("gpsimd", dg2[1], dg2[1].n)
    ei("gpsimd", lambda e: e.collective_compute(
        "AllGather", ALU.bypass, replica_groups=GROUPS,
        ins=[kq_piece], outs=[kq_gath]), inc=colKQ)

    wt("sync", dw[4], dw[4].n)
    wt("sync", dw[5], dw[5].n)
    wv_war = cP.n
    for dc in range(NDC):
        dma("sync", bufB[:, 2048 + 1024 * dc:2048 + 1024 * dc + 1024],
            wvb[128 * dc:128 * dc + 128, :], dw[4 + dc % 2],
            wait=(cP, wv_war) if dc == 0 else None)
    dv4, dv5 = dw[4].n, dw[5].n
    wt("tensor", dw[5], dv5)
    for t128 in range(16):
        for half in range(2):
            idx = t128 * 2 + half
            pb = ps[idx % 2]
            wt("tensor", cA, slotdep.get(("psKQ", idx % 2), 0))
            for dc in range(NDC):
                ei("tensor", lambda e, pb=pb, dc=dc, t128=t128, half=half: e.matmul(
                    pb[:, :],
                    ybf[:, 2048 * dc + 128 * t128:2048 * dc + 128 * t128 + 128],
                    bufB[:, 2048 + 1024 * dc + 512 * half:2048 + 1024 * dc + 512 * half + 512],
                    start=(dc == 0), stop=(dc == NDC - 1), skip_group_check=True),
                    wait=(dw[4], dv4) if dc == 0 else None,
                    inc=(cP if dc == NDC - 1 else None))
            ss = 10240 + 512 * (idx % 2)
            wt("scalar", dg2[idx % 2], slotdep.get(("sskq", idx % 2), 0))
            a = ei("scalar", lambda e, pb=pb, ss=ss: e.copy(bufB[:, ss:ss + 512], pb[:, :]),
                   wait=(cP, cP.n), inc=cA)
            slotdep[("psKQ", idx % 2)] = a
            g = dma("gpsimd", v_piece[128 * t128:128 * t128 + 128, 512 * half:512 * half + 512],
                    bufB[:, ss:ss + 512], dg2[idx % 2], wait=(cA, a))
            slotdep[("sskq", idx % 2)] = g
    wt("gpsimd", dg2[0], dg2[0].n)
    wt("gpsimd", dg2[1], dg2[1].n)
    ei("gpsimd", lambda e: e.collective_compute(
        "AllGather", ALU.bypass, replica_groups=GROUPS,
        ins=[v_piece], outs=[v_gath]), inc=colV)

    wt("gpsimd", colKQ, 1)
    wt("gpsimd", colV, 1)

    Gc = {}

    def gval(mult):
        if "g" not in Gc:
            Gc["g"] = nc.partition_id([ET.Pool]) % 4
        if mult not in Gc:
            Gc[mult] = Gc["g"] * mult
        return Gc[mult]

    for hp in range(2):
        if hp == 1:
            wt("gpsimd", cP, cP.n)
        for hh in range(2):
            for which in range(2):
                def mk(hh=hh, hp=hp, which=which):
                    def fn(e):
                        c0 = 1024 * which + 64 * (2 * hp + hh)
                        src = kq_gath.rearrange("(r q) t -> q r t", q=2048)[
                            c0:2048, :, :][bass.ds(gval(256), 64), :, :]
                        dst = bufA[64 * hh:64 * hh + 64,
                                   8192 * which:8192 * which + 8192
                                   ].rearrange("p (r t) -> p r t", r=4)
                        return e.dma_start(out=dst, in_=src, bounds_check=None)
                    return fn
                bb.ei("gpsimd", mk(), inc=dG, inc_n=16)
            for ah in range(2):
                def mkv(hh=hh, hp=hp, ah=ah):
                    def fn(e):
                        c0 = 64 * (2 * hp + hh)
                        src = v_gath.rearrange("(a p) d -> p a d", p=128)[
                            :, 32 * ah:32 * ah + 32, c0:1024][:, :, bass.ds(gval(256), 64)]
                        dst = bufB[:, 4096 * hh + 2048 * ah:4096 * hh + 2048 * ah + 2048
                                   ].rearrange("p (a d) -> p a d", a=32)
                        return e.dma_start(out=dst, in_=src, bounds_check=None)
                    return fn
                bb.ei("gpsimd", mkv(), inc=dG, inc_n=16)
        kqv_loaded = dG.n

        wt("scalar", dG, kqv_loaded)
        wt("tensor", cV, cV.n)
        vml = 0
        for which in range(2):
            for chunk in range(16):
                base = 8192 * which + 512 * chunk
                sq = 10240 + 512 * (chunk % 2)
                wt("scalar", cP, slotdep.get(("msq", chunk % 2), 0))
                a = ei("scalar", lambda e, base=base, sq=sq: e.activation(
                    bufB[:, sq:sq + 512], bufA[:, base:base + 512], AF.Square), inc=cA)
                wt("tensor", cV, vml)
                ei("tensor", lambda e, sq=sq: e.matmul(
                    ps[5][0:32, :], bsm[:, 288:320], bufB[:, sq:sq + 512],
                    start=True, stop=True, skip_group_check=True, tile_position=(0, 0)),
                    wait=(cA, a))
                p = ei("tensor", lambda e, sq=sq: e.matmul(
                    ps[6][0:32, :], bsm[:, 320:352], bufB[:, sq:sq + 512],
                    start=True, stop=True, skip_group_check=True, tile_position=(0, 0)),
                    inc=cP)
                slotdep[("msq", chunk % 2)] = p
                cA_ = 3104 + 16 * (2 * which) + chunk
                cB_ = 3104 + 16 * (2 * which + 1) + chunk
                ei("vector", lambda e, cA_=cA_: e.reduce_max(
                    stat[0:32, cA_:cA_ + 1], ps[5][0:32, :], AX.X), wait=(cP, p))
                vml = ei("vector", lambda e, cB_=cB_: e.reduce_max(
                    stat[0:32, cB_:cB_ + 1], ps[6][0:32, :], AX.X), inc=cV)
        for g in range(4):
            vml = ei("vector", lambda e, g=g: e.reduce_max(
                stat[0:32, 3072 + g:3073 + g], stat[0:32, 3104 + 16 * g:3120 + 16 * g],
                AX.X), wait=(cV, vml) if g == 0 else None, inc=(cV if g == 3 else None))
        ei("vector", lambda e: e.tensor_mul(stat[0:1, 3077:3078], stat[0:1, 3072:3073],
                                            stat[0:1, 3074:3075]), wait=(cV, vml))
        v = ei("vector", lambda e: e.tensor_mul(stat[0:1, 3078:3079], stat[0:1, 3073:3074],
                                                stat[0:1, 3075:3076]), inc=cV)
        a = ei("scalar", lambda e: e.activation(stat[0:1, 3077:3079], stat[0:1, 3077:3079],
                                                AF.Sqrt), wait=(cV, v), inc=cA)
        pA = ei("tensor", lambda e: e.matmul(
            ps[5][:, 2:3], fsm[0:1, 192:320], stat[0:1, 3077:3078],
            start=True, stop=True, skip_group_check=True), wait=(cA, a))
        p = ei("tensor", lambda e: e.matmul(
            ps[5][:, 3:4], fsm[0:1, 192:320], stat[0:1, 3078:3079],
            start=True, stop=True, skip_group_check=True), inc=cP)
        wt("vector", cA, cA.n)
        ei("vector", lambda e: e.tensor_scalar_mul(fsm[:, 50:51], ps[5][:, 2:3], -SCALE),
           wait=(cP, p))
        mhat_v = ei("vector", lambda e: e.tensor_scalar_mul(fsm[:, 51:52], ps[5][:, 3:4],
                                                            -SCALE), inc=cV)
        wt("scalar", cV, mhat_v)
        wt("tensor", cV, mhat_v)

        psO, psL = ps[0], ps[1]
        for qr in range(16):
            q0 = 512 * qr
            kmax = 4 * qr + 4
            for c in range(kmax):
                diag = (c >= 4 * qr)
                qs = 128 * (c - 4 * qr) if diag else 0
                width = 512 - qs
                sa = ps[2 + (c % 2)]
                sb = ps[4 + (c % 2)]
                ei("tensor", lambda e, sa=sa, c=c, q0=q0, qs=qs, width=width: e.matmul(
                    sa[:, qs:qs + width], bufA[0:64, 128 * c:128 * c + 128],
                    bufA[0:64, 8192 + q0 + qs:8192 + q0 + qs + width],
                    start=True, stop=True, skip_group_check=True, tile_position=(0, 0)),
                    wait=(cA, slotdep.get(("sbank", c % 2), 0)))
                p = ei("tensor", lambda e, sb=sb, c=c, q0=q0, qs=qs, width=width: e.matmul(
                    sb[:, qs:qs + width], bufA[64:128, 128 * c:128 * c + 128],
                    bufA[64:128, 8192 + q0 + qs:8192 + q0 + qs + width],
                    start=True, stop=True, skip_group_check=True, tile_position=(64, 0)),
                    inc=cP)
                ptA = 8192 + 512 * (c % 2)
                ptB = 9216 + 512 * (c % 2)
                a0 = ei("scalar", lambda e, sa=sa, ptA=ptA, qs=qs, width=width: e.activation(
                    bufB[:, ptA + qs:ptA + qs + width], sa[:, qs:qs + width], AF.Exp,
                    bias=fsm[:, 50:51], scale=SCALE),
                    wait=(cP, max(p, slotdep.get(("pt", c % 2), 0))), inc=cA)
                a = ei("scalar", lambda e, sb=sb, ptB=ptB, qs=qs, width=width: e.activation(
                    bufB[:, ptB + qs:ptB + qs + width], sb[:, qs:qs + width], AF.Exp,
                    bias=fsm[:, 51:52], scale=SCALE), inc=cA)
                slotdep[("sbank", c % 2)] = a
                if diag:
                    ei("vector", lambda e, ptA=ptA, qs=qs: e.tensor_mul(
                        bufB[:, ptA + qs:ptA + qs + 128], bufB[:, ptA + qs:ptA + qs + 128],
                        bsm[:, 352:480]), wait=(cA, a))
                    v = ei("vector", lambda e, ptB=ptB, qs=qs: e.tensor_mul(
                        bufB[:, ptB + qs:ptB + qs + 128], bufB[:, ptB + qs:ptB + qs + 128],
                        bsm[:, 352:480]), inc=cV)
                    pv_wait = (cV, v)
                else:
                    pv_wait = (cA, a)
                first = (c == 0)
                last = (c == kmax - 1)
                if first:
                    wt("tensor", cV, slotdep.get("psOL", 0))
                ei("tensor", lambda e, ptA=ptA, c=c, qs=qs, width=width, first=first,
                   last=last: e.matmul(
                       psO[0:64, qs:qs + width], bufB[:, 64 * c:64 * c + 64],
                       bufB[:, ptA + qs:ptA + qs + width],
                       start=first, stop=last, skip_group_check=True, tile_position=(0, 0)),
                   wait=pv_wait)
                ei("tensor", lambda e, ptB=ptB, c=c, qs=qs, width=width, first=first,
                   last=last: e.matmul(
                       psO[64:128, qs:qs + width], bufB[:, 4096 + 64 * c:4096 + 64 * c + 64],
                       bufB[:, ptB + qs:ptB + qs + width],
                       start=first, stop=last, skip_group_check=True, tile_position=(0, 64)))
                ei("tensor", lambda e, ptA=ptA, qs=qs, width=width, first=first,
                   last=last: e.matmul(
                       psL[0:32, qs:qs + width], bsm[:, 256:288],
                       bufB[:, ptA + qs:ptA + qs + width],
                       start=first, stop=last, skip_group_check=True, tile_position=(0, 0)))
                p = ei("tensor", lambda e, ptB=ptB, qs=qs, width=width, first=first,
                       last=last: e.matmul(
                           psL[32:64, qs:qs + width], bsm[:, 256:288],
                           bufB[:, ptB + qs:ptB + qs + width],
                           start=first, stop=last, skip_group_check=True,
                           tile_position=(0, 32)), inc=cP)
                slotdep[("pt", c % 2)] = p
            pv_done = cP.n
            v = ei("vector", lambda e: e.reciprocal(stat[0:64, 1024:1536], psL[0:64, :]),
                   wait=(cP, pv_done), inc=cV)
            p = ei("tensor", lambda e: e.matmul(
                ps[6][:, :], fsm[0:64, 64:192], stat[0:64, 1024:1536],
                start=True, stop=True, skip_group_check=True), wait=(cV, v), inc=cP)
            vc = ei("vector", lambda e: e.tensor_copy(stat[:, 1536:2048], ps[6][:, :]),
                    wait=(cP, p), inc=cV)
            v = ei("vector", lambda e, hp=hp, q0=q0: e.tensor_mul(
                ybf[:, 8192 * hp + q0:8192 * hp + q0 + 512], psO[:, :],
                stat[:, 1536:2048]), wait=(cV, vc), inc=cV)
            slotdep["psOL"] = v

    attn_oml = cV.n
    wt("gpsimd", cP, cP.n)
    for half in range(2):
        def mkw(half=half):
            def fn(e):
                return e.dma_start(
                    out=bufB[:, 1024 * half:1024 * half + 1024],
                    in_=wob[128 * half:1024, :][bass.ds(gval(256), 128), :],
                    bounds_check=None)
            return fn
        bb.ei("gpsimd", mkw(), inc=dG, inc_n=16)
    dwo = dG.n
    wt("gpsimd", dg2[0], dg2[0].n)
    wt("gpsimd", dg2[1], dg2[1].n)
    wt("tensor", dG, dwo)
    wt("tensor", cV, attn_oml)
    for dc in range(NDC):
        for tseg in range(16):
            t0 = 512 * tseg
            idx = dc * 16 + tseg
            pb = ps[2 + (idx % 2)]
            wt("tensor", cA, slotdep.get(("psP", idx % 2), 0))
            for half in range(2):
                ei("tensor", lambda e, pb=pb, half=half, dc=dc, t0=t0: e.matmul(
                    pb[:, :], bufB[:, 1024 * half + 128 * dc:1024 * half + 128 * dc + 128],
                    ybf[:, 8192 * half + t0:8192 * half + t0 + 512],
                    start=(half == 0), stop=(half == 1), skip_group_check=True),
                    inc=(cP if half == 1 else None))
            tsl = 1024 + 512 * (idx % 2)
            wt("scalar", dg2[idx % 2], slotdep.get(("sskq", idx % 2), 0))
            a = ei("scalar", lambda e, pb=pb, tsl=tsl: e.copy(stat[:, tsl:tsl + 512],
                                                              pb[:, :]),
                   wait=(cP, cP.n), inc=cA)
            slotdep[("psP", idx % 2)] = a
            g = dma("gpsimd",
                    rs_in[1024 * (tseg // 4) + 128 * dc:1024 * (tseg // 4) + 128 * dc + 128,
                          512 * (tseg % 4):512 * (tseg % 4) + 512],
                    stat[:, tsl:tsl + 512], dg2[idx % 2], wait=(cA, a))
            slotdep[("sskq", idx % 2)] = g
    wt("gpsimd", dg2[0], dg2[0].n)
    wt("gpsimd", dg2[1], dg2[1].n)
    ei("gpsimd", lambda e: e.collective_compute(
        "ReduceScatter", ALU.add, replica_groups=GROUPS,
        ins=[rs_in], outs=[rs_out]), inc=colO)
    wt("gpsimd", colO, 1)
    for dc in range(NDC):
        g = dma("gpsimd", wstg[:, :], rs_out[128 * dc:128 * dc + 128, :], dG,
                wait=(cV, slotdep.get("rsadd", attn_oml)))
        v = ei("vector", lambda e, dc=dc: e.tensor_add(
            xr[:, 2048 * dc:2048 * dc + 2048], xr[:, 2048 * dc:2048 * dc + 2048],
            wstg[:, :]), wait=(dG, g), inc=cV)
        slotdep["rsadd"] = v
    x2_done = cV.n

    if STAGE == 2:
        emit_output(x2_done)
        finalize(bb, st)
        return nc

    wt("vector", cV, x2_done)
    wt("vector", dstc, dstc.n)
    y3_ready = run_LN()
    for s in range(4):
        slotdep[("wq1", s)] = cP.n
    for s in range(2):
        slotdep[("wq2", s)] = cP.n
        slotdep[("psF", s)] = cA.n
        slotdep[("psG", s)] = cA.n
        slotdep[("tsl", s)] = cV.n
    x3_done = emit_FFN(y3_ready)
    emit_output(x3_done)
    finalize(bb, st)
    return nc


def finalize(bb, st):
    nc = bb.nc
    with nc.Block() as block:
        @block.sync
        def _(e):
            for fn in bb.plan["sync"]:
                fn(e)

        @block.vector
        def _(e):
            for fn in bb.plan["vector"]:
                fn(e)

        @block.tensor
        def _(e):
            for fn in bb.plan["tensor"]:
                fn(e)

        @block.scalar
        def _(e):
            for fn in bb.plan["scalar"]:
                fn(e)

        @block.gpsimd
        def _(e):
            for fn in bb.plan["gpsimd"]:
                fn(e)
    st.close()


def _prep_shared(L0_attn_w_qkv, L0_attn_w_o, L0_ffn_W1, L0_ffn_b1, L0_ffn_W2, L0_ffn_b2):
    wqkv = np.asarray(L0_attn_w_qkv, dtype=np.float32)
    cols = wqkv.reshape(D, NH, 3, 64)
    w_q = np.ascontiguousarray(cols[:, :, 0, :].reshape(D, D))
    w_k = np.ascontiguousarray(cols[:, :, 1, :].reshape(D, D))
    w_v = np.ascontiguousarray(cols[:, :, 2, :].reshape(D, D))
    w_o = np.asarray(L0_attn_w_o, dtype=np.float32)

    def stream_tile(w):
        return np.ascontiguousarray(
            w.reshape(8, 128, 8, 128).transpose(2, 1, 0, 3).reshape(D, D))

    d = {
        "W1": np.asarray(L0_ffn_W1, dtype=np.float32),
        "W2": np.asarray(L0_ffn_W2, dtype=np.float32),
        "b1_rs": np.ascontiguousarray(np.asarray(L0_ffn_b1, np.float32).reshape(32, 128).T),
        "b2_rs": np.ascontiguousarray(np.asarray(L0_ffn_b2, np.float32).reshape(8, 128).T),
        "wk_st": stream_tile(w_k),
        "wq_st": stream_tile(w_q),
        "wv_st": np.ascontiguousarray(w_v),
        "wo_st": w_o,
        "ident": np.eye(128, dtype=ml_dtypes.bfloat16),
        "ones128": np.ones((128, 128), dtype=ml_dtypes.bfloat16),
        "ones32": np.ones((128, 32), dtype=ml_dtypes.bfloat16),
        "ones64A": np.concatenate([np.ones((64, 32)), np.zeros((64, 32))]
                                  ).astype(ml_dtypes.bfloat16),
        "ones64B": np.concatenate([np.zeros((64, 32)), np.ones((64, 32))]
                                  ).astype(ml_dtypes.bfloat16),
        "mask01": np.where(np.arange(128)[:, None] > np.arange(128)[None, :], 0.0, 1.0
                           ).astype(ml_dtypes.bfloat16),
        "selbc": np.zeros((64, 128), dtype=np.float32),
        "onesbc": np.ones((1, 128), dtype=np.float32),
        "onescol": np.ones((128, 1), dtype=np.float32),
    }
    d["selbc"][0, 0:64] = 1.0
    d["selbc"][32, 64:128] = 1.0
    return d


def kernel(L0_attn_x, L0_attn_w_qkv, L0_attn_w_o, L0_ffn_W1, L0_ffn_b1, L0_ffn_W2, L0_ffn_b2):
    x = np.asarray(L0_attn_x, dtype=np.float32)
    shared = _prep_shared(L0_attn_w_qkv, L0_attn_w_o, L0_ffn_W1, L0_ffn_b1,
                          L0_ffn_W2, L0_ffn_b2)
    nc = build()
    in_maps = []
    for c in range(8):
        b, g = c // 4, c % 4
        m = dict(shared)
        m["x_fm"] = np.ascontiguousarray(x[b, T * g:T * (g + 1), :].T)
        in_maps.append(m)
    res = run_bass_kernel_spmd(nc, in_maps, core_ids=list(range(8)))
    out = np.empty((2, S, D), dtype=np.float32)
    for c in range(8):
        b, g = c // 4, c % 4
        out[b, T * g:T * (g + 1), :] = res.results[c]["out_fm"].T
    return out
